# version 14
# baseline (speedup 1.0000x reference)
import numpy as np
import ml_dtypes

import concourse.bass as bass
import concourse.mybir as mybir
from concourse.bass_utils import run_bass_kernel_spmd

F32 = mybir.dt.float32
BF16 = mybir.dt.bfloat16
I32 = mybir.dt.int32
AF = mybir.ActivationFunctionType
ALU = mybir.AluOpType
AX = mybir.AxisListType

K = 1024
H = 4096
S = 1024
FF = 9984
KC = H // 128
SC = S // 128
FC = FF // 128
NCH = H // 512
FGROUPS = [16, 16, 16, 16, 14]
NFR = 4
EPS = 1e-5
THD = 2.5
CLAMP = 1e-30


class Sem:
    def __init__(self, nc, name):
        self.h = nc.alloc_semaphore(name=name)
        self.n = 0

    def bump(self, k=1):
        self.n += k
        return self.n


class Prog:
    ENGS = ("tensor", "vector", "scalar", "sync", "gpsimd")

    def __init__(self):
        self.steps = {e: [] for e in self.ENGS}

    def on(self, eng, fn):
        self.steps[eng].append(fn)

    def wait(self, eng, sem, val):
        if val > 0:
            self.on(eng, lambda e, s=sem.h, v=val: e.wait_ge(s, v))

    def waitt(self, eng, sv):
        if sv is not None:
            self.wait(eng, sv[0], sv[1])

    def emit(self, nc):
        with nc.Block() as block:
            @block.tensor
            def _(t):
                for fn in self.steps["tensor"]:
                    fn(t)

            @block.vector
            def _(v):
                for fn in self.steps["vector"]:
                    fn(v)

            @block.scalar
            def _(s):
                for fn in self.steps["scalar"]:
                    fn(s)

            @block.sync
            def _(s):
                for fn in self.steps["sync"]:
                    fn(s)

            @block.gpsimd
            def _(g):
                for fn in self.steps["gpsimd"]:
                    fn(g)


class Ring:

    def __init__(self, nc, name, nslots, dma_eng, prog):
        self.sems = [Sem(nc, f"{name}{i}") for i in range(nslots)]
        self.nslots = nslots
        self.gate = [[] for _ in range(nslots)]
        self.eng = dma_eng
        self.p = prog

    def load(self, slot, pairs):
        sem = self.sems[slot]
        gates = self.gate[slot]
        self.gate[slot] = []

        def fn(e, prs=list(pairs), g=list(gates), sh=sem.h):
            for gs, gv in g:
                if gv > 0:
                    e.wait_ge(gs.h, gv)
            for dst, src in prs:
                e.dma_start(out=dst, in_=src).then_inc(sh, 16)

        self.p.on(self.eng, fn)
        sem.bump(16 * len(pairs))
        return sem.n

    def release(self, slot, prog_sem, val):
        self.gate[slot].append((prog_sem, val))


def build_program(R=1024):
    TT = R // 128
    assert R % 128 == 0 and (TT % 2 == 0 or TT == 1)
    NHALF = 2 if TT >= 2 else 1
    THALF = TT // NHALF
    TOKH = THALF * 128
    TL = R // NHALF
    assert TL <= 512

    nc = bass.Bass("TRN2", target_bir_lowering=False, detect_race_conditions=False)

    dx = nc.dram_tensor("x", [R, H], F32, kind="ExternalInput").ap()
    dout = nc.dram_tensor("out", [R, H], F32, kind="ExternalOutput").ap()
    dup = nc.dram_tensor("up_h", [SC, 128, KC * 128], F32, kind="ExternalInput").ap()
    dgt = nc.dram_tensor("gate_h", [SC, 128, KC * 128], F32, kind="ExternalInput").ap()
    dfuse = nc.dram_tensor("fuse_h", [NCH * SC, 128, 7 * 512], F32, kind="ExternalInput").ap()
    dgf = nc.dram_tensor("gatef_h", [FC, 128, KC * 128], F32, kind="ExternalInput").ap()
    duf = nc.dram_tensor("upf_h", [FC, 128, KC * 128], F32, kind="ExternalInput").ap()
    ddn = nc.dram_tensor("down_h", [NCH, 128, FC * 512], F32, kind="ExternalInput").ap()
    dW = nc.dram_tensor("W_h", [NCH, 128, KC * 512], F32, kind="ExternalInput").ap()
    dbias = nc.dram_tensor("bias_bc", [128, H], F32, kind="ExternalInput").ap()
    dlA = nc.dram_tensor("lA_h", [128, 3 * KC * 16], F32, kind="ExternalInput").ap()
    dlB = nc.dram_tensor("lB_h", [16, 3 * H], F32, kind="ExternalInput").ap()
    dlsi = nc.dram_tensor("lsi", [1, 4], I32, kind="ExternalInput").ap()
    dbvec = nc.dram_tensor("bvec", [1, 1], F32, kind="ExternalInput").ap()
    dident = nc.dram_tensor("ident", [128, 128], F32, kind="ExternalInput").ap()
    dlp = nc.dram_tensor("lp_h", [128, SC], F32, kind="ExternalInput").ap()
    ddf = nc.dram_tensor("df_h", [128, 6], F32, kind="ExternalInput").ap()

    base = (nc.sbuf_base + 31) & ~31

    def sb(name, shape, dtype, off_b):
        off = base + int(off_b)
        assert off % 32 == 0, (name, off)
        return nc.alloc_sbuf_tensor_at(name, shape, dtype, offset=off)

    h_T = sb("h_T", [128, KC * R], BF16, 0)
    frng = sb("fring", [128, NFR * 7 * 512], BF16, 0)
    dterm = sb("dterm", [128, 3 * R], F32, 28 * K)
    tmpb = sb("tmpb", [128, 3 * R], BF16, 40 * K)
    dabs2 = sb("dabs2", [128, R], F32, 46 * K)
    keepk = sb("keepk", [128, R], BF16, 50 * K)
    h_T2 = sb("h_T2", [128, KC * R], BF16, 0)
    h2tok = sb("h2tok", [128, H], F32, 0)
    wslB = sb("wslB", [128, 2 * 16 * 512], BF16, 0)
    tt4 = sb("tt4", [128, H], F32, 16 * K)
    outt = sb("outt", [128, H], F32, 32 * K)
    x0t3 = sb("x0t3", [128, H], F32, 32 * K)
    lAt = sb("lAt", [128, 3 * KC * 16], BF16, 48 * K)
    t1s = sb("t1s", [16, 3 * 512], BF16, 51 * K + 512)
    junkB = sb("junkB", [128, H], BF16, 54 * K + 1024)
    ya = sb("ya", [128, TT * H], BF16, 64 * K)
    x0t = sb("x0t", [128, 2 * H], F32, 64 * K)
    htok = sb("htok", [128, 2 * H], F32, 96 * K)
    wupg = sb("wupg", [128, 4 * KC * 128], BF16, 64 * K)
    dtile = sb("dtile", [128, 2 * R], F32, 96 * K)
    dabs = sb("dabs", [128, R], F32, 104 * K)
    sgnt = sb("sgnt", [128, R], F32, 108 * K)
    junkA = sb("junkA", [128, H], BF16, 128 * K)
    logd = sb("logd", [128, SC * R], F32, 128 * K)
    au = sb("au", [128, SC * R], BF16, 128 * K + 32 * R)
    aus = sb("aus", [128, SC * R], BF16, 128 * K + 48 * R)
    x0t2 = sb("x0t2", [128, 2 * H], F32, 128 * K)
    htok2 = sb("htok2", [128, 2 * H], F32, 160 * K)
    vg = sb("vg", [128, 16 * R], BF16, 128 * K)
    wgu = sb("wgu", [128, 4 * KC * 128], BF16, 160 * K)
    drng = sb("drng", [128, 2 * 16 * 512], BF16, 160 * K)
    gst = sb("gst", [128, 2 * R], BF16, 192 * K)
    h2T = sb("h2T", [128, KC * TOKH], BF16, 128 * K)
    y2h = sb("y2h", [128, THALF * H], BF16, 160 * K)
    bkr = sb("bkr", [16, 2 * 3 * 512], BF16, 192 * K)
    biasb = sb("biasb", [128, H], BF16, 198 * K)
    ident = sb("identt", [128, 128], F32, 206 * K)
    lp = sb("lp", [128, SC], F32, 206 * K + 512)
    dfr = sb("dfr", [128, 6], F32, 206 * K + 544)
    dfn = sb("dfn", [128, 6], F32, 206 * K + 576)
    lsii = sb("lsii", [1, 4], I32, 206 * K + 608)
    lsif = sb("lsif", [1, 4], F32, 206 * K + 640)
    bvt = sb("bvt", [1, 1], F32, 206 * K + 672)
    smt = sb("smt", [1, 8], F32, 206 * K + 704)
    s16 = sb("s16", [16, 3], F32, 206 * K + 736)
    ones16 = sb("ones16", [1, 16], F32, 206 * K + 768)
    czer = sb("czer", [128, 2], F32, 206 * K + 832)
    nc.const_aps.register(F32, 0.0, czer[:, 0:1])
    nc.const_aps.register(F32, EPS, czer[:, 1:2])
    stat = sb("stat", [128, 8], F32, 206 * K + 864)
    mean1 = sb("mean1", [128, TT], F32, 206 * K + 896)
    inv1 = sb("inv1", [128, TT], F32, 206 * K + 928)
    bmni1 = sb("bmni1", [128, TT], F32, 206 * K + 960)
    yasum = sb("yasum", [128, TT * NCH], F32, 206 * K + 992)
    x2sum = sb("x2sum", [128, TT], F32, 206 * K + 1248)
    y2sum = sb("y2sum", [128, THALF * NCH], F32, 206 * K + 1280)

    ps = [nc.alloc_psum_tensor(f"pb{i}", [128, 512], F32) for i in range(8)]

    p = Prog()
    pe = Sem(nc, "pe")
    vec = Sem(nc, "vec")
    act = Sem(nc, "act")
    bar = Sem(nc, "bar")
    rx = Ring(nc, "rx", 2, "sync", p)
    rw = Ring(nc, "rw", 2, "gpsimd", p)
    rf = Ring(nc, "rf", NFR, "gpsimd", p)
    rd = Ring(nc, "rd", 2, "gpsimd", p)
    rb = Ring(nc, "rb", 2, "gpsimd", p)
    rk2 = Ring(nc, "rk2", 2, "gpsimd", p)
    ro = Ring(nc, "ro", 1, "sync", p)
    rm = Sem(nc, "rm")

    fence_n = [0]

    def fence():
        fence_n[0] += 1
        v = 5 * fence_n[0]
        for e in Prog.ENGS:
            p.on(e, lambda en, bh=bar.h: en.sem_inc(bh, 1))
            p.on(e, lambda en, bh=bar.h, vv=v: en.wait_ge(bh, vv))

    def mm(out_ap, l_ap, r_ap, start, stop, inc=False):
        if inc:
            pe.bump()

        def fn(t, o=out_ap, l=l_ap, r=r_ap, s0=start, s1=stop, ic=inc, sh=pe.h):
            i = t.matmul(o, l, r, start=s0, stop=s1)
            if ic:
                i.then_inc(sh, 1)
        p.on("tensor", fn)

    def v_op(fb, inc=False):
        if inc:
            vec.bump()

        def fn(v, f=fb, ic=inc, sh=vec.h):
            i = f(v)
            if ic:
                i.then_inc(sh, 1)
        p.on("vector", fn)

    def a_op(fb, inc=False):
        if inc:
            act.bump()

        def fn(a, f=fb, ic=inc, sh=act.h):
            i = f(a)
            if ic:
                i.then_inc(sh, 1)
        p.on("scalar", fn)

    def vdr():
        p.on("vector", lambda v: v.drain())

    bank_ev = [None] * 8

    def bank_wait(eng, b):
        p.waitt(eng, bank_ev[b])

    def ln_stats(src_ap, junk, agate=None, sums=None):
        if sums is None:
            v_op(lambda v, s=src_ap: v.reduce_sum(stat[:, 0:1], s, AX.X, negate=True))
        else:
            v_op(lambda v, sm=sums: v.reduce_sum(stat[:, 0:1], sm, AX.X, negate=True))
        v_op(lambda v, s=src_ap, j=junk: v.scalar_tensor_tensor(
            j[:], s, 1.0, s, ALU.mult, ALU.mult, accum_out=stat[:, 1:2]))
        v_op(lambda v: v.tensor_scalar_mul(stat[:, 2:3], stat[:, 0:1], 1.0 / H))
        vdr()
        v_op(lambda v: v.tensor_scalar(stat[:, 3:4], stat[:, 2:3], stat[:, 2:3], None,
                                       ALU.mult))
        v_op(lambda v: v.tensor_scalar_mul(stat[:, 4:5], stat[:, 1:2], 1.0 / H))
        vdr()
        v_op(lambda v: v.tensor_scalar(stat[:, 4:5], stat[:, 4:5], stat[:, 3:4], None,
                                       ALU.subtract))
        vdr()
        v_op(lambda v: v.tensor_scalar(stat[:, 4:5], stat[:, 4:5], EPS, None, ALU.add),
             inc=True)
        p.wait("scalar", vec, vec.n)
        a_op(lambda a: a.activation(stat[:, 7:8], stat[:, 4:5], AF.Sqrt), inc=True)
        p.wait("vector", act, act.n)
        v_op(lambda v: v.reciprocal(stat[:, 5:6], stat[:, 7:8]))
        vdr()
        v_op(lambda v: v.tensor_scalar(stat[:, 6:7], stat[:, 2:3], stat[:, 5:6], None,
                                       ALU.mult), inc=True)
        vdr()

    for dst, src in ((ident[:], dident), (lp[:], dlp), (dfr[:], ddf),
                     (lsii[:], dlsi), (bvt[:], dbvec)):
        def fn(g, d=dst, s=src, sh=rm.h):
            g.dma_start(out=d, in_=s).then_inc(sh, 16)
        p.on("gpsimd", fn)
        rm.bump(16)
    p.wait("vector", rm, rm.n)
    v_op(lambda v: v.memset(czer[:, 0:1], 0.0))
    v_op(lambda v: v.memset(czer[:, 1:2], EPS))
    v_op(lambda v: v.memset(ones16[:], 1.0))
    v_op(lambda v: v.tensor_scalar_mul(dfn[:], dfr[:], -1.0))
    v_op(lambda v: v.tensor_copy(lsif[:], lsii[:]))
    vdr()
    v_op(lambda v: v.tensor_scalar(smt[:, 0:4], lsif[:], bvt[0:1, 0:1], None, ALU.is_le))
    vdr()
    v_op(lambda v: v.reduce_sum(smt[:, 4:5], smt[:, 0:4], AX.X))
    vdr()
    v_op(lambda v: v.tensor_scalar(smt[:, 5:6], smt[:, 4:5], 1.0, None, ALU.subtract))
    vdr()
    v_op(lambda v: v.tensor_scalar(smt[:, 6:7], smt[:, 5:6], 0.0, None, ALU.is_gt))
    v_op(lambda v: v.tensor_scalar(smt[:, 7:8], smt[:, 5:6], 1.0, 0.0, ALU.subtract, ALU.max))
    vdr()
    v_op(lambda v: v.tensor_scalar(smt[:, 7:8], smt[:, 7:8], 2.0, None, ALU.min))
    vdr()
    for kk in range(3):
        v_op(lambda v, k_=kk: v.tensor_scalar(smt[:, k_:k_ + 1], smt[:, 7:8], float(k_),
                                              None, ALU.is_equal))
        vdr()
        v_op(lambda v, k_=kk: v.tensor_scalar(smt[:, k_:k_ + 1], smt[:, k_:k_ + 1],
                                              smt[:, 6:7], None, ALU.mult), inc=(kk == 2))
        vdr()
    p.wait("tensor", vec, vec.n)
    p.wait("tensor", rm, rm.n)
    mm(ps[0][0:16, 0:3], ones16[0:1, :], smt[0:1, 0:3], True, True, inc=True)
    p.wait("vector", pe, pe.n)
    v_op(lambda v: v.tensor_copy(s16[:], ps[0][0:16, 0:3]), inc=True)
    fence()

    def h_build(dst, x0buf, hbuf, with_stats, invs, bmnis, save, tiles, junk,
                tokn, single_hbuf=False):
        hb_gate = [0, 0]
        for idx, tg in enumerate(tiles):
            slot = idx % 2
            fill = rx.load(slot, [(x0buf[:, slot * H:(slot + 1) * H],
                                   dx[tg * 128:(tg + 1) * 128, :])])
            xs = x0buf[:, slot * H:(slot + 1) * H]
            p.wait("vector", rx.sems[slot], fill)
            if with_stats:
                p.wait("scalar", rx.sems[slot], fill)
                ln_stats(xs, junk)
                if save:
                    v_op(lambda v, c=tg: v.tensor_copy(mean1[:, c:c + 1], stat[:, 2:3]))
                    v_op(lambda v, c=tg: v.tensor_copy(inv1[:, c:c + 1], stat[:, 5:6]))
                    v_op(lambda v, c=tg: v.tensor_copy(bmni1[:, c:c + 1], stat[:, 6:7]))
            hslot = idx % 2
            hb = hbuf[:, hslot * H:(hslot + 1) * H]
            if hb_gate[hslot] > 0:
                p.wait("vector", pe, hb_gate[hslot])
            if with_stats:
                v_op(lambda v, hh=hb, xx=xs: v.tensor_scalar(
                    hh, xx, stat[:, 2:3], stat[:, 5:6], ALU.add, ALU.mult), inc=True)
            else:
                v_op(lambda v, hh=hb, xx=xs, iv=invs[:, tg:tg + 1], bm=bmnis[:, tg:tg + 1]:
                     v.tensor_scalar(hh, xx, iv, bm, ALU.mult, ALU.add), inc=True)
            rx.release(slot, vec, vec.n)
            p.wait("tensor", vec, vec.n)
            for b in range(8):
                bank_wait("tensor", b)
                for j in range(4):
                    kk = b * 4 + j
                    def tr(t, o=ps[b][:, j * 128:(j + 1) * 128],
                           i_=hb[:, kk * 128:(kk + 1) * 128]):
                        return t.transpose(o, i_, ident[:])
                    if j < 3:
                        p.on("tensor", tr)
                    else:
                        pe.bump()
                        p.on("tensor", lambda t, f=tr, sh=pe.h: f(t).then_inc(sh, 1))
                p.wait("scalar", pe, pe.n)
                toff = idx * 128 if tokn == TOKH else tg * 128
                def ev(a, d_=dst, b_=b, to=toff, tn=tokn,
                       src=ps[b][:].rearrange("p (j t) -> p j t", j=4)):
                    dv = d_[:].rearrange("p (k t) -> p k t", t=tn)[:, b_ * 4:(b_ + 1) * 4,
                                                                  to:to + 128]
                    return a.activation(dv, src, AF.Copy)
                a_op(ev, inc=True)
                bank_ev[b] = (act, act.n)
            hb_gate[hslot] = pe.n

    def h_build_b0(tg, idx, x2buf):
        xs = x2buf[:, tg * H:(tg + 1) * H]
        ln_stats(xs, junkB, sums=x2sum[:, tg:tg + 1])
        if b0_gate[0] > 0:
            p.wait("vector", pe, b0_gate[0])
        v_op(lambda v, xx=xs: v.tensor_scalar(
            h2tok[:], xx, stat[:, 2:3], stat[:, 5:6], ALU.add, ALU.mult), inc=True)
        p.wait("tensor", vec, vec.n)
        for b in range(8):
            bank_wait("tensor", b)
            for j in range(4):
                kk = b * 4 + j
                def tr(t, o=ps[b][:, j * 128:(j + 1) * 128],
                       i_=h2tok[:, kk * 128:(kk + 1) * 128]):
                    return t.transpose(o, i_, ident[:])
                if j < 3:
                    p.on("tensor", tr)
                else:
                    pe.bump()
                    p.on("tensor", lambda t, f=tr, sh=pe.h: f(t).then_inc(sh, 1))
            p.wait("scalar", pe, pe.n)
            def ev(a, b_=b, to=idx * 128,
                   src=ps[b][:].rearrange("p (j t) -> p j t", j=4)):
                dv = h2T[:].rearrange("p (k t) -> p k t", t=TOKH)[:, b_ * 4:(b_ + 1) * 4,
                                                                 to:to + 128]
                return a.activation(dv, src, AF.Copy)
            a_op(ev, inc=True)
            bank_ev[b] = (act, act.n)
        b0_gate[0] = pe.n
        p.wait("tensor", rm, rm_b)
        p.wait("tensor", act, act.n)
        for a_i in range(3):
            bb = 4 + a_i
            bank_wait("tensor", bb)
            for k in range(KC):
                mm(ps[bb][0:16, 0:128], lAt[:, (a_i * KC + k) * 16:(a_i * KC + k + 1) * 16],
                   h2T[:, k * TOKH + idx * 128: k * TOKH + (idx + 1) * 128],
                   k == 0, k == KC - 1, inc=(k == KC - 1))
            p.wait("vector", pe, pe.n)
            v_op(lambda v, a_=a_i, b_=bb, i_=idx: v.tensor_scalar(
                t1s[:, a_ * 512 + i_ * 128: a_ * 512 + (i_ + 1) * 128],
                ps[b_][0:16, 0:128], s16[:, a_:a_ + 1], None, ALU.mult), inc=True)
            bank_ev[bb] = (vec, vec.n)
        t1_done[idx] = vec.n

    h_build(h_T, x0t, htok, True, None, None, True, list(range(TT)), junkA, R)
    rw.gate[0].append((vec, vec.n))
    a1_pref = rw.load(0, [(wupg[:, 0:H], dup[0]), (wupg[:, H:2 * H], dgt[0])])
    fence()

    dt_gate = [0, 0]
    for mS in range(SC):
        slot = mS % 2
        if mS == 0:
            wf = a1_pref
        else:
            wf = rw.load(slot, [
                (wupg[:, (2 * slot) * H:(2 * slot + 1) * H], dup[mS]),
                (wupg[:, (2 * slot + 1) * H:(2 * slot + 2) * H], dgt[mS]),
            ])
        par = (mS % 2) * 4
        bau = [par + 0, par + 1]
        bgt = [par + 2, par + 3]
        p.wait("tensor", rw.sems[slot], wf)
        for b in bau[:NHALF] + bgt[:NHALF]:
            bank_wait("tensor", b)
        wup_ap = wupg[:, (2 * slot) * H:(2 * slot + 1) * H]
        wgt_ap = wupg[:, (2 * slot + 1) * H:(2 * slot + 2) * H]
        for k in range(KC):
            for hi in range(NHALF):
                mm(ps[bau[hi]][:, 0:TL], wup_ap[:, k * 128:(k + 1) * 128],
                   h_T[:, k * R + hi * TL: k * R + (hi + 1) * TL],
                   k == 0, k == KC - 1, inc=(k == KC - 1))
            for hi in range(NHALF):
                mm(ps[bgt[hi]][:, 0:TL], wgt_ap[:, k * 128:(k + 1) * 128],
                   h_T[:, k * R + hi * TL: k * R + (hi + 1) * TL],
                   k == 0, k == KC - 1, inc=(k == KC - 1))
        rw.release(slot, pe, pe.n)
        p.wait("vector", pe, pe.n)
        for hi in range(NHALF):
            v_op(lambda v, b=bau[hi], o=au[:, mS * R + hi * TL: mS * R + (hi + 1) * TL]:
                 v.tensor_copy(o, ps[b][:, 0:TL]))
        dsl = mS % 2
        dta = dtile[:, dsl * R:(dsl + 1) * R]
        if dt_gate[dsl] > 0:
            p.wait("vector", act, dt_gate[dsl])
        for hi in range(NHALF):
            v_op(lambda v, b=bgt[hi], o=dta[:, hi * TL:(hi + 1) * TL], m=mS:
                 v.tensor_scalar(o, ps[b][:, 0:TL], lp[:, m:m + 1], None, ALU.subtract),
                 inc=(hi == NHALF - 1))
        for b in bau[:NHALF] + bgt[:NHALF]:
            bank_ev[b] = (vec, vec.n)
        p.wait("scalar", vec, vec.n)
        a_op(lambda a, d=dta: a.activation(dabs[:, 0:R], d, AF.Abs), inc=True)
        dt_gate[dsl] = act.n
        v_op(lambda v, d=dta: v.tensor_scalar(sgnt[:, 0:R], d, 0.0, 2.0, ALU.is_gt, ALU.mult))
        v_op(lambda v, m=mS: v.scalar_tensor_tensor(
            aus[:, m * R:(m + 1) * R], sgnt[:, 0:R], 1.0,
            au[:, m * R:(m + 1) * R], ALU.subtract, ALU.mult))
        p.wait("vector", act, act.n)
        v_op(lambda v: v.tensor_scalar(dabs[:, 0:R], dabs[:, 0:R], CLAMP, None, ALU.max),
             inc=True)
        p.wait("scalar", vec, vec.n)
        a_op(lambda a, m=mS: a.activation(logd[:, m * R:(m + 1) * R], dabs[:, 0:R], AF.Ln),
             inc=True)
    pe_a1_done = pe.n
    fuse_fill = {}
    for li in range(NFR - 1):
        rf.gate[li % NFR].append((pe, pe_a1_done))
        fuse_fill[li] = rf.load(li % NFR, [(frng[:, (li % NFR) * 3584:((li % NFR) + 1) * 3584],
                                            dfuse[li])])
    fence()

    bank_ev = [None] * 8
    bank_rd = [0] * 8
    tb_gate = [0, 0, 0]
    dt_gate2 = [0, 0, 0]
    kk_gate = [0]
    tcount = [0]
    for n in range(NCH):
        for k in range(SC):
            li = n * SC + k
            slot = li % NFR
            pre = li + NFR - 1
            if pre < NCH * SC and pre not in fuse_fill:
                fuse_fill[pre] = rf.load(pre % NFR,
                                         [(frng[:, (pre % NFR) * 3584:((pre % NFR) + 1) * 3584],
                                           dfuse[pre])])
            fill = fuse_fill[li]
            for g in range(7):
                if g == 0:
                    src = au[:, k * R:(k + 1) * R]
                    p.wait("tensor", rf.sems[slot], fill)
                else:
                    m = g
                    tsl = tcount[0] % 3
                    tcount[0] += 1
                    dts = dterm[:, tsl * R:(tsl + 1) * R]
                    tbs = tmpb[:, tsl * R:(tsl + 1) * R]
                    if dt_gate2[tsl] > 0:
                        p.wait("scalar", vec, dt_gate2[tsl])
                    if m == 5:
                        if kk_gate[0] > 0:
                            p.wait("scalar", vec, kk_gate[0])
                        a_op(lambda a, i_=logd[:, k * R:(k + 1) * R]:
                             a.activation(dabs2[:], i_, AF.Exp), inc=True)
                    a_op(lambda a, o=dts, i_=logd[:, k * R:(k + 1) * R], m_=m, g_=g:
                         a.activation(o, i_, AF.Exp, bias=dfn[:, g_ - 1:g_],
                                      scale=float(m_)), inc=True)
                    p.wait("vector", act, act.n)
                    if m == 5:
                        v_op(lambda v: v.tensor_scalar(keepk[:], dabs2[:], THD, None,
                                                       ALU.is_le))
                    if tb_gate[tsl] > 0:
                        p.wait("vector", pe, tb_gate[tsl])
                    bv = aus if (m % 2 == 1) else au
                    v_op(lambda v, o=tbs, d=dts, b_=bv[:, k * R:(k + 1) * R]:
                         v.tensor_tensor(o, d, b_, ALU.mult), inc=(m <= 4))
                    if m > 4:
                        v_op(lambda v, o=tbs:
                             v.tensor_tensor(o, o, keepk[:], ALU.mult), inc=True)
                        if m == 6:
                            kk_gate[0] = vec.n
                    dt_gate2[tsl] = vec.n
                    src = tbs
                    p.wait("tensor", vec, vec.n)
                last_gk = (k == SC - 1 and g == 6)
                for m_t in range(TT):
                    if k == 0 and g == 0:
                        bank_wait("tensor", m_t)
                    mm(ps[m_t][:], src[:, m_t * 128:(m_t + 1) * 128],
                       frng[:, slot * 3584 + g * 512: slot * 3584 + (g + 1) * 512],
                       (k == 0 and g == 0), last_gk,
                       inc=(last_gk or m_t == TT - 1))
                    if last_gk:
                        bank_rd[m_t] = pe.n
                if g > 0:
                    tb_gate[tsl] = pe.n
            rf.release(slot, pe, pe.n)
        for m_t in range(TT):
            p.wait("vector", pe, bank_rd[m_t])
            v_op(lambda v, b=m_t, n_=n:
                 v.tensor_copy(ya[:, b * H + n_ * 512:(b * H + (n_ + 1) * 512)], ps[b][:]),
                 inc=True)
            bank_ev[m_t] = (vec, vec.n)
    fence()

    bank_ev = [None] * 8
    h_build(h_T2, x0t2, htok2, False, inv1, bmni1, False, list(range(TT)), junkA, R)
    pe_hb = pe.n
    rw.gate[0].append((pe, pe_hb))
    wgu_pref = rw.load(0, [
        (wgu[:, 0:H], dgf[0]),
        (wgu[:, H:2 * H], duf[0]),
    ])
    fence()

    fc0 = 0
    rel0_pe = [0]
    rd_rel0 = [0]
    rd_pref = [None]
    for ig, gsz in enumerate(FGROUPS):
        gs_gate = [0, 0]
        for mf in range(gsz):
            slot = mf % 2
            if mf == 0 and wgu_pref is not None:
                wf = wgu_pref
                wgu_pref = None
            else:
                wf = rw.load(slot, [
                    (wgu[:, (2 * slot) * H:(2 * slot + 1) * H], dgf[fc0 + mf]),
                    (wgu[:, (2 * slot + 1) * H:(2 * slot + 2) * H], duf[fc0 + mf]),
                ])
            par = (mf % 2) * 2
            bg = [par + 0, par + 1]
            bu = [par + 4, par + 5]
            p.wait("tensor", rw.sems[slot], wf)
            for b in bg[:NHALF] + bu[:NHALF]:
                bank_wait("tensor", b)
            wg_ap = wgu[:, (2 * slot) * H:(2 * slot + 1) * H]
            wu_ap = wgu[:, (2 * slot + 1) * H:(2 * slot + 2) * H]
            for k in range(KC):
                for hi in range(NHALF):
                    mm(ps[bg[hi]][:, 0:TL], wg_ap[:, k * 128:(k + 1) * 128],
                       h_T2[:, k * R + hi * TL: k * R + (hi + 1) * TL],
                       k == 0, k == KC - 1, inc=(k == KC - 1))
            peg_g = pe.n
            for k in range(KC):
                for hi in range(NHALF):
                    mm(ps[bu[hi]][:, 0:TL], wu_ap[:, k * 128:(k + 1) * 128],
                       h_T2[:, k * R + hi * TL: k * R + (hi + 1) * TL],
                       k == 0, k == KC - 1, inc=(k == KC - 1))
            rw.release(slot, pe, pe.n)
            if slot == 0:
                rel0_pe[0] = pe.n
            peg = pe.n
            gsl = mf % 2
            if gs_gate[gsl] > 0:
                p.wait("scalar", vec, gs_gate[gsl])
            p.wait("scalar", pe, peg_g)
            for hi in range(NHALF):
                a_op(lambda a, b=bg[hi], o=gst[:, gsl * R + hi * TL: gsl * R + (hi + 1) * TL]:
                     a.activation(o, ps[b][:, 0:TL], AF.Silu), inc=(hi == NHALF - 1))
            p.wait("vector", act, act.n)
            p.wait("vector", pe, peg)
            for hi in range(NHALF):
                v_op(lambda v, b=bu[hi], m=mf, h_=hi, g_=gsl:
                     v.scalar_tensor_tensor(
                         vg[:, m * R + h_ * TL: m * R + (h_ + 1) * TL],
                         ps[b][:, 0:TL], 1.0,
                         gst[:, g_ * R + h_ * TL: g_ * R + (h_ + 1) * TL],
                         ALU.mult, ALU.mult),
                     inc=(hi == NHALF - 1))
            gs_gate[gsl] = vec.n
            for b in bg[:NHALF] + bu[:NHALF]:
                bank_ev[b] = (vec, vec.n)
        rd.gate[0].append((pe, rel0_pe[0]))
        rd_pref[0] = rd.load(0, [(drng[:, 0: gsz * 512],
                                  ddn[0][:, fc0 * 512:(fc0 + gsz) * 512])])
        fence()
        for n in range(NCH):
            slot = n % 2
            if n == 0:
                fill = rd_pref[0]
            else:
                fill = rd.load(slot, [(drng[:, slot * 8192: slot * 8192 + gsz * 512],
                                       ddn[n][:, fc0 * 512:(fc0 + gsz) * 512])])
            p.wait("tensor", rd.sems[slot], fill)
            for kf in range(gsz):
                for m_t in range(TT):
                    if kf == 0:
                        bank_wait("tensor", m_t)
                    mm(ps[m_t][:], vg[:, kf * R + m_t * 128: kf * R + (m_t + 1) * 128],
                       drng[:, slot * 8192 + kf * 512: slot * 8192 + (kf + 1) * 512],
                       kf == 0, kf == gsz - 1,
                       inc=(kf == gsz - 1 or m_t == TT - 1))
                    if kf == gsz - 1:
                        bank_rd[m_t] = pe.n
            rd.release(slot, pe, pe.n)
            last_grp = (ig == len(FGROUPS) - 1)
            for m_t in range(TT):
                p.wait("vector", pe, bank_rd[m_t])
                v_op(lambda v, b=m_t, n_=n, lg=last_grp:
                     v.scalar_tensor_tensor(
                         ya[:, b * H + n_ * 512: b * H + (n_ + 1) * 512],
                         ps[b][:], 1.0,
                         ya[:, b * H + n_ * 512: b * H + (n_ + 1) * 512],
                         ALU.mult, ALU.add,
                         accum_out=(yasum[:, b * NCH + n_: b * NCH + n_ + 1]
                                    if lg else None)),
                     inc=True)
                bank_ev[m_t] = (vec, vec.n)
            if n == NCH - 2:
                rd_rel0[0] = pe.n
        fc0 += gsz
        if ig < len(FGROUPS) - 1:
            rw.gate[0].append((pe, rd_rel0[0]))
            wgu_pref = rw.load(0, [
                (wgu[:, 0:H], dgf[fc0]),
                (wgu[:, H:2 * H], duf[fc0]),
            ])
        fence()
    pe_p2 = pe.n

    for dst, src in ((lAt[:], dlA), (biasb[:], dbias)):
        def fn(g, d=dst, s=src, sh=rm.h):
            g.dma_start(out=d, in_=s).then_inc(sh, 16)
        p.on("gpsimd", fn)
        rm.bump(16)
    rm_b = rm.n
    t1_done = [0] * THALF

    x2 = ya
    bank_ev = [None] * 8
    b0_gate = [0]
    x3_gate = [0]

    def p3_tile(t):
        fill = rx.load(0, [(x0t3[:], dx[t * 128:(t + 1) * 128, :])])
        yat = ya[:, t * H:(t + 1) * H]
        ln_stats(yat, junkB, sums=yasum[:, t * NCH:(t + 1) * NCH])
        p.wait("vector", rx.sems[0], fill)
        v_op(lambda v: v.tensor_scalar(x0t3[:], x0t3[:], stat[:, 6:7], None, ALU.add))
        vdr()
        v_op(lambda v, y=yat, t_=t: v.scalar_tensor_tensor(
            y, y, stat[:, 5:6], x0t3[:], ALU.mult, ALU.add,
            accum_out=x2sum[:, t_:t_ + 1]), inc=True)
        x3_gate[0] = vec.n
        rx.release(0, vec, vec.n)

    for t in range(THALF):
        p3_tile(t)
        h_build_b0(t, t, x2)

    b2_last = [0]
    for hf in range(NHALF):
        tls = list(range(hf * THALF, (hf + 1) * THALF))
        rb.gate[1].append((pe, pe_p2) if hf == 0 else (vec, b2_last[0]))
        rb.gate[0].append((pe, b0_gate[0]))
        bank_ev_b = [bank_ev[b] for b in range(8)]
        first_evac = True
        p3_next = [NHALF == 2 and hf == 0, THALF]
        for n in range(NCH):
            kfill = rk2.load(n % 2, [(bkr[:, (n % 2) * 1536:(n % 2) * 1536 + 1536],
                                      dlB.rearrange("p (a h) -> p a h", a=3)[:, :, n * 512:(n + 1) * 512])])
            for khalf in range(2):
                slot = khalf ^ 1
                fill = rb.load(slot, [(wslB[:, slot * 8192:(slot + 1) * 8192],
                                       dW[n][:, khalf * 8192:(khalf + 1) * 8192])])
                p.wait("tensor", rb.sems[slot], fill)
                for m_t in range(THALF):
                    b = m_t + (n % 2) * 4
                    if n == 0 and khalf == 0:
                        p.wait("tensor", vec, t1_done[m_t])
                    if khalf == 0 and bank_ev_b[b] is not None:
                        p.waitt("tensor", bank_ev_b[b])
                    for k in range(16):
                        kg = khalf * 16 + k
                        mm(ps[b][:], h2T[:, kg * TOKH + m_t * 128: kg * TOKH + (m_t + 1) * 128],
                           wslB[:, slot * 8192 + k * 512: slot * 8192 + (k + 1) * 512],
                           kg == 0, False,
                           inc=(m_t == THALF - 1 and k == 15))
                rb.release(slot, pe, pe.n)
            p.wait("tensor", rk2.sems[n % 2], kfill)
            for m_t in range(THALF):
                b = m_t + (n % 2) * 4
                for a_i in range(3):
                    mm(ps[b][:], t1s[0:16, a_i * 512 + m_t * 128: a_i * 512 + (m_t + 1) * 128],
                       bkr[0:16, (n % 2) * 1536 + a_i * 512:(n % 2) * 1536 + (a_i + 1) * 512],
                       False, a_i == 2,
                       inc=(a_i == 2))
                bank_rd[b] = pe.n
            rk2.release(n % 2, pe, pe.n)
            if first_evac:
                p.wait("vector", rm, rm_b)
                first_evac = False
            for m_t in range(THALF):
                b = m_t + (n % 2) * 4
                p.wait("vector", pe, bank_rd[b])
                v_op(lambda v, b_=b, m=m_t, n_=n:
                     v.scalar_tensor_tensor(
                         y2h[:, m * H + n_ * 512: m * H + (n_ + 1) * 512],
                         ps[b_][:], 1.0, biasb[:, n_ * 512:(n_ + 1) * 512],
                         ALU.mult, ALU.add,
                         accum_out=y2sum[:, m * NCH + n_: m * NCH + n_ + 1]),
                     inc=True)
                bank_ev_b[b] = (vec, vec.n)
            if p3_next[0] and n % 2 == 0 and p3_next[1] < TT:
                p3_tile(p3_next[1])
                p3_next[1] += 1
        y2_done = vec.n
        bank_ev = [None] * 8
        b0_gate = [0]
        wsl_done = pe.n
        for idx, tg in enumerate(tls):
            y2t = y2h[:, idx * H:(idx + 1) * H]
            ln_stats(y2t, junkB, sums=y2sum[:, idx * NCH:(idx + 1) * NCH])
            v_op(lambda v, y=y2t, xs=x2[:, tg * H:(tg + 1) * H]:
                 v.tensor_scalar(tt4[:], xs, stat[:, 6:7], None, ALU.add))
            vdr()
            if ro.sems[0].n > 0:
                p.wait("vector", ro.sems[0], ro.sems[0].n)
            v_op(lambda v, y=y2t:
                 v.scalar_tensor_tensor(outt[:], y, stat[:, 5:6], tt4[:],
                                        ALU.mult, ALU.add), inc=True)
            b2_last[0] = vec.n
            p.wait("sync", vec, vec.n)
            sem = ro.sems[0]

            def st(e, o=dout[tg * 128:(tg + 1) * 128, :], i_=outt[:], sh=sem.h):
                e.dma_start(out=o, in_=i_).then_inc(sh, 16)
            p.on("sync", st)
            sem.bump(16)
            if hf == 0 and NHALF == 2:
                if idx == 0:
                    p.wait("vector", pe, wsl_done)
                h_build_b0(THALF + idx, idx, x2)

    p.wait("sync", ro.sems[0], ro.sems[0].n)

    p.emit(nc)
    return nc



def _prep_shared(i):
    up_w = np.asarray(i["L0_attn_up_w"], np.float32)
    gate_w = np.asarray(i["L0_attn_gate_w"], np.float32)
    lao = np.asarray(i["L0_attn_local_approx_output"], np.float32)
    fuse = np.asarray(i["L0_attn_fuse_weight"], np.float32)
    gfw = np.asarray(i["L0_attn_ffn_gate_w"], np.float32)
    ufw = np.asarray(i["L0_attn_ffn_up_w"], np.float32)
    dfw = np.asarray(i["L0_attn_ffn_down_w"], np.float32)
    W = np.asarray(i["L0_ffn_weight"], np.float32)
    bias = np.asarray(i["L0_ffn_bias"], np.float32)
    lA = np.asarray(i["L0_ffn_lora_A"], np.float32)
    lB = np.asarray(i["L0_ffn_lora_B"], np.float32)
    lsi = np.asarray(i["L0_ffn_lora_start_indices"]).astype(np.int32).reshape(1, 4)
    lp = np.asarray(i["L0_attn_local_point"], np.float32)
    df = np.asarray(i["L0_attn_discount_factor"], np.float32)

    d = {}
    d["up_h"] = np.ascontiguousarray(
        up_w.reshape(SC, 128, KC, 128).transpose(0, 3, 2, 1).reshape(SC, 128, KC * 128))
    d["gate_h"] = np.ascontiguousarray(
        gate_w.reshape(SC, 128, KC, 128).transpose(0, 3, 2, 1).reshape(SC, 128, KC * 128))
    F_all = np.stack([lao.T] + [fuse[:, :, g].T for g in range(6)])
    d["fuse_h"] = np.ascontiguousarray(
        F_all.reshape(7, SC, 128, NCH, 512).transpose(3, 1, 2, 0, 4)
        .reshape(NCH * SC, 128, 7 * 512))
    d["gatef_h"] = np.ascontiguousarray(
        gfw.reshape(FC, 128, KC, 128).transpose(0, 3, 2, 1).reshape(FC, 128, KC * 128))
    d["upf_h"] = np.ascontiguousarray(
        ufw.reshape(FC, 128, KC, 128).transpose(0, 3, 2, 1).reshape(FC, 128, KC * 128))
    d["down_h"] = np.ascontiguousarray(
        dfw.T.reshape(FC, 128, NCH, 512).transpose(2, 1, 0, 3).reshape(NCH, 128, FC * 512))
    d["W_h"] = np.ascontiguousarray(
        W.T.reshape(KC, 128, NCH, 512).transpose(2, 1, 0, 3).reshape(NCH, 128, KC * 512))
    d["bias_bc"] = np.ascontiguousarray(np.broadcast_to(bias, (128, H)))
    d["lA_h"] = np.ascontiguousarray(
        lA.reshape(3, 16, KC, 128).transpose(3, 0, 2, 1).reshape(128, 3 * KC * 16))
    d["lB_h"] = np.ascontiguousarray(lB.transpose(2, 0, 1).reshape(16, 3 * H))
    d["lsi"] = lsi
    d["ident"] = np.eye(128, dtype=np.float32)
    d["lp_h"] = np.ascontiguousarray(lp.reshape(SC, 128).T)
    d["df_h"] = np.ascontiguousarray(np.broadcast_to(df, (128, 6)))
    return d


def make_in_maps(inputs, n_cores, rows_per_core):
    x = np.asarray(inputs["L0_attn_hidden_states"], np.float32)
    B, T, _ = x.shape
    xf = np.ascontiguousarray(x.reshape(B * T, H))
    shared = _prep_shared(inputs)
    in_maps = []
    for c in range(n_cores):
        m = dict(shared)
        r0 = c * rows_per_core
        m["x"] = np.ascontiguousarray(xf[r0:r0 + rows_per_core])
        m["bvec"] = np.full((1, 1), float(r0 // T), np.float32)
        in_maps.append(m)
    return in_maps


def kernel(**inputs):
    n_cores = 8
    R = 1024
    in_maps = make_in_maps(inputs, n_cores, R)
    nc = build_program(R)
    res = run_bass_kernel_spmd(nc, in_maps, core_ids=list(range(n_cores)))
    out = np.concatenate([np.asarray(res.results[c]["out"]) for c in range(n_cores)],
                         axis=0)
    return out.reshape(2, 4096, 4096).astype(np.float32, copy=False)


# version 15 (final)
# speedup vs baseline: 1.0507x; 1.0507x over previous
import numpy as np
import ml_dtypes

import concourse.bass as bass
import concourse.mybir as mybir
from concourse.bass_utils import run_bass_kernel_spmd

F32 = mybir.dt.float32
BF16 = mybir.dt.bfloat16
I32 = mybir.dt.int32
AF = mybir.ActivationFunctionType
ALU = mybir.AluOpType
AX = mybir.AxisListType

K = 1024
H = 4096
S = 1024
FF = 9984
KC = H // 128
SC = S // 128
FC = FF // 128
NCH = H // 512
FGROUPS = [16, 16, 16, 16, 14]
NFR = 4
EPS = 1e-5
THD = 2.5
CLAMP = 1e-30


class Sem:
    def __init__(self, nc, name):
        self.h = nc.alloc_semaphore(name=name)
        self.n = 0

    def bump(self, k=1):
        self.n += k
        return self.n


class Prog:
    ENGS = ("tensor", "vector", "scalar", "sync", "gpsimd")

    def __init__(self):
        self.steps = {e: [] for e in self.ENGS}

    def on(self, eng, fn):
        self.steps[eng].append(fn)

    def wait(self, eng, sem, val):
        if val > 0:
            self.on(eng, lambda e, s=sem.h, v=val: e.wait_ge(s, v))

    def waitt(self, eng, sv):
        if sv is not None:
            self.wait(eng, sv[0], sv[1])

    def emit(self, nc):
        with nc.Block() as block:
            @block.tensor
            def _(t):
                for fn in self.steps["tensor"]:
                    fn(t)

            @block.vector
            def _(v):
                for fn in self.steps["vector"]:
                    fn(v)

            @block.scalar
            def _(s):
                for fn in self.steps["scalar"]:
                    fn(s)

            @block.sync
            def _(s):
                for fn in self.steps["sync"]:
                    fn(s)

            @block.gpsimd
            def _(g):
                for fn in self.steps["gpsimd"]:
                    fn(g)


class Ring:

    def __init__(self, nc, name, nslots, dma_eng, prog):
        self.sems = [Sem(nc, f"{name}{i}") for i in range(nslots)]
        self.nslots = nslots
        self.gate = [[] for _ in range(nslots)]
        self.eng = dma_eng
        self.p = prog

    def load(self, slot, pairs):
        sem = self.sems[slot]
        gates = self.gate[slot]
        self.gate[slot] = []

        def fn(e, prs=list(pairs), g=list(gates), sh=sem.h):
            for gs, gv in g:
                if gv > 0:
                    e.wait_ge(gs.h, gv)
            for dst, src in prs:
                e.dma_start(out=dst, in_=src).then_inc(sh, 16)

        self.p.on(self.eng, fn)
        sem.bump(16 * len(pairs))
        return sem.n

    def release(self, slot, prog_sem, val):
        self.gate[slot].append((prog_sem, val))


def build_program(R=1024):
    TT = R // 128
    assert R % 128 == 0 and (TT % 2 == 0 or TT == 1)
    NHALF = 2 if TT >= 2 else 1
    THALF = TT // NHALF
    TOKH = THALF * 128
    TL = R // NHALF
    assert TL <= 512

    nc = bass.Bass("TRN2", target_bir_lowering=False, detect_race_conditions=False)

    dx = nc.dram_tensor("x", [R, H], F32, kind="ExternalInput").ap()
    dout = nc.dram_tensor("out", [R, H], F32, kind="ExternalOutput").ap()
    dup = nc.dram_tensor("up_h", [SC, 128, KC * 128], F32, kind="ExternalInput").ap()
    dgt = nc.dram_tensor("gate_h", [SC, 128, KC * 128], F32, kind="ExternalInput").ap()
    dfuse = nc.dram_tensor("fuse_h", [NCH * SC, 128, 7 * 512], F32, kind="ExternalInput").ap()
    dgf = nc.dram_tensor("gatef_h", [FC, 128, KC * 128], F32, kind="ExternalInput").ap()
    duf = nc.dram_tensor("upf_h", [FC, 128, KC * 128], F32, kind="ExternalInput").ap()
    ddn = nc.dram_tensor("down_h", [NCH, 128, FC * 512], F32, kind="ExternalInput").ap()
    dW = nc.dram_tensor("W_h", [NCH, 128, KC * 512], F32, kind="ExternalInput").ap()
    dbias = nc.dram_tensor("bias_bc", [128, H], F32, kind="ExternalInput").ap()
    dlA = nc.dram_tensor("lA_h", [128, 3 * KC * 16], F32, kind="ExternalInput").ap()
    dlB = nc.dram_tensor("lB_h", [16, 3 * H], F32, kind="ExternalInput").ap()
    dlsi = nc.dram_tensor("lsi", [1, 4], I32, kind="ExternalInput").ap()
    dbvec = nc.dram_tensor("bvec", [1, 1], F32, kind="ExternalInput").ap()
    dident = nc.dram_tensor("ident", [128, 128], F32, kind="ExternalInput").ap()
    dlp = nc.dram_tensor("lp_h", [128, SC], F32, kind="ExternalInput").ap()
    ddf = nc.dram_tensor("df_h", [128, 6], F32, kind="ExternalInput").ap()

    base = (nc.sbuf_base + 31) & ~31

    def sb(name, shape, dtype, off_b):
        off = base + int(off_b)
        assert off % 32 == 0, (name, off)
        return nc.alloc_sbuf_tensor_at(name, shape, dtype, offset=off)

    h_T = sb("h_T", [128, KC * R], BF16, 0)
    frng = sb("fring", [128, NFR * 7 * 512], BF16, 0)
    dterm = sb("dterm", [128, 3 * R], F32, 28 * K)
    tmpb = sb("tmpb", [128, 3 * R], BF16, 40 * K)
    dabs2 = sb("dabs2", [128, R], F32, 46 * K)
    keepk = sb("keepk", [128, R], BF16, 50 * K)
    h_T2 = sb("h_T2", [128, KC * R], BF16, 0)
    h2tok = sb("h2tok", [128, H], F32, 0)
    wslB = sb("wslB", [128, 2 * 16 * 512], BF16, 0)
    tt4 = sb("tt4", [128, H], F32, 16 * K)
    outt = sb("outt", [128, H], F32, 32 * K)
    x0t3 = sb("x0t3", [128, H], F32, 32 * K)
    lAt = sb("lAt", [128, 3 * KC * 16], BF16, 48 * K)
    t1s = sb("t1s", [16, 3 * 512], BF16, 51 * K + 512)
    junkB = sb("junkB", [128, H], BF16, 54 * K + 1024)
    ya = sb("ya", [128, TT * H], BF16, 64 * K)
    x0t = sb("x0t", [128, 2 * H], F32, 64 * K)
    htok = sb("htok", [128, 2 * H], F32, 96 * K)
    wupg = sb("wupg", [128, 4 * KC * 128], BF16, 64 * K)
    dtile = sb("dtile", [128, 2 * R], F32, 96 * K)
    dabs = sb("dabs", [128, R], F32, 104 * K)
    sgnt = sb("sgnt", [128, R], F32, 108 * K)
    junkA = sb("junkA", [128, H], BF16, 128 * K)
    logd = sb("logd", [128, SC * R], F32, 128 * K)
    au = sb("au", [128, SC * R], BF16, 128 * K + 32 * R)
    aus = sb("aus", [128, SC * R], BF16, 128 * K + 48 * R)
    x0t2 = sb("x0t2", [128, 2 * H], F32, 128 * K)
    htok2 = sb("htok2", [128, 2 * H], F32, 160 * K)
    vg = sb("vg", [128, 16 * R], BF16, 128 * K)
    wgu = sb("wgu", [128, 4 * KC * 128], BF16, 160 * K)
    drng = sb("drng", [128, 2 * 16 * 512], BF16, 160 * K)
    gst = sb("gst", [128, 2 * R], BF16, 192 * K)
    h2T = sb("h2T", [128, KC * TOKH], BF16, 128 * K)
    y2h = sb("y2h", [128, THALF * H], BF16, 160 * K)
    bkr = sb("bkr", [16, 2 * 3 * 512], BF16, 192 * K)
    biasb = sb("biasb", [128, H], BF16, 198 * K)
    ident = sb("identt", [128, 128], F32, 206 * K)
    lp = sb("lp", [128, SC], F32, 206 * K + 512)
    dfr = sb("dfr", [128, 6], F32, 206 * K + 544)
    dfn = sb("dfn", [128, 6], F32, 206 * K + 576)
    lsii = sb("lsii", [1, 4], I32, 206 * K + 608)
    lsif = sb("lsif", [1, 4], F32, 206 * K + 640)
    bvt = sb("bvt", [1, 1], F32, 206 * K + 672)
    smt = sb("smt", [1, 8], F32, 206 * K + 704)
    s16 = sb("s16", [16, 3], F32, 206 * K + 736)
    ones16 = sb("ones16", [1, 16], F32, 206 * K + 768)
    czer = sb("czer", [128, 2], F32, 206 * K + 832)
    nc.const_aps.register(F32, 0.0, czer[:, 0:1])
    nc.const_aps.register(F32, EPS, czer[:, 1:2])
    stat = sb("stat", [128, 8], F32, 206 * K + 864)
    mean1 = sb("mean1", [128, TT], F32, 206 * K + 896)
    inv1 = sb("inv1", [128, TT], F32, 206 * K + 928)
    bmni1 = sb("bmni1", [128, TT], F32, 206 * K + 960)
    yasum = sb("yasum", [128, TT * NCH], F32, 206 * K + 992)
    x2sum = sb("x2sum", [128, TT], F32, 206 * K + 1248)
    y2sum = sb("y2sum", [128, THALF * NCH], F32, 206 * K + 1280)

    ps = [nc.alloc_psum_tensor(f"pb{i}", [128, 512], F32) for i in range(8)]

    p = Prog()
    pe = Sem(nc, "pe")
    vec = Sem(nc, "vec")
    act = Sem(nc, "act")
    bar = Sem(nc, "bar")
    rx = Ring(nc, "rx", 2, "sync", p)
    rw = Ring(nc, "rw", 2, "gpsimd", p)
    rf = Ring(nc, "rf", NFR, "gpsimd", p)
    rd = Ring(nc, "rd", 2, "gpsimd", p)
    rb = Ring(nc, "rb", 2, "gpsimd", p)
    rk2 = Ring(nc, "rk2", 2, "gpsimd", p)
    ro = Ring(nc, "ro", 1, "sync", p)
    rm = Sem(nc, "rm")

    fence_n = [0]

    def fence():
        fence_n[0] += 1
        v = 5 * fence_n[0]
        for e in Prog.ENGS:
            p.on(e, lambda en, bh=bar.h: en.sem_inc(bh, 1))
            p.on(e, lambda en, bh=bar.h, vv=v: en.wait_ge(bh, vv))

    def mm(out_ap, l_ap, r_ap, start, stop, inc=False):
        if inc:
            pe.bump()

        def fn(t, o=out_ap, l=l_ap, r=r_ap, s0=start, s1=stop, ic=inc, sh=pe.h):
            i = t.matmul(o, l, r, start=s0, stop=s1)
            if ic:
                i.then_inc(sh, 1)
        p.on("tensor", fn)

    def v_op(fb, inc=False):
        if inc:
            vec.bump()

        def fn(v, f=fb, ic=inc, sh=vec.h):
            i = f(v)
            if ic:
                i.then_inc(sh, 1)
        p.on("vector", fn)

    def a_op(fb, inc=False):
        if inc:
            act.bump()

        def fn(a, f=fb, ic=inc, sh=act.h):
            i = f(a)
            if ic:
                i.then_inc(sh, 1)
        p.on("scalar", fn)

    def vdr():
        p.on("vector", lambda v: v.drain())

    bank_ev = [None] * 8

    def bank_wait(eng, b):
        p.waitt(eng, bank_ev[b])

    def ln_stats(src_ap, junk, agate=None, sums=None):
        if sums is None:
            v_op(lambda v, s=src_ap: v.reduce_sum(stat[:, 0:1], s, AX.X, negate=True))
        else:
            v_op(lambda v, sm=sums: v.reduce_sum(stat[:, 0:1], sm, AX.X, negate=True))
        v_op(lambda v, s=src_ap, j=junk: v.scalar_tensor_tensor(
            j[:], s, 1.0, s, ALU.mult, ALU.mult, accum_out=stat[:, 1:2]))
        v_op(lambda v: v.tensor_scalar_mul(stat[:, 2:3], stat[:, 0:1], 1.0 / H))
        vdr()
        v_op(lambda v: v.tensor_scalar(stat[:, 3:4], stat[:, 2:3], stat[:, 2:3], None,
                                       ALU.mult))
        v_op(lambda v: v.tensor_scalar_mul(stat[:, 4:5], stat[:, 1:2], 1.0 / H))
        vdr()
        v_op(lambda v: v.tensor_scalar(stat[:, 4:5], stat[:, 4:5], stat[:, 3:4], None,
                                       ALU.subtract))
        vdr()
        v_op(lambda v: v.tensor_scalar(stat[:, 4:5], stat[:, 4:5], EPS, None, ALU.add),
             inc=True)
        p.wait("scalar", vec, vec.n)
        a_op(lambda a: a.activation(stat[:, 7:8], stat[:, 4:5], AF.Sqrt), inc=True)
        p.wait("vector", act, act.n)
        v_op(lambda v: v.reciprocal(stat[:, 5:6], stat[:, 7:8]))
        vdr()
        v_op(lambda v: v.tensor_scalar(stat[:, 6:7], stat[:, 2:3], stat[:, 5:6], None,
                                       ALU.mult), inc=True)
        vdr()

    for dst, src in ((ident[:], dident), (lp[:], dlp), (dfr[:], ddf),
                     (lsii[:], dlsi), (bvt[:], dbvec)):
        def fn(g, d=dst, s=src, sh=rm.h):
            g.dma_start(out=d, in_=s).then_inc(sh, 16)
        p.on("gpsimd", fn)
        rm.bump(16)
    p.wait("vector", rm, rm.n)
    v_op(lambda v: v.memset(czer[:, 0:1], 0.0))
    v_op(lambda v: v.memset(czer[:, 1:2], EPS))
    v_op(lambda v: v.memset(ones16[:], 1.0))
    v_op(lambda v: v.tensor_scalar_mul(dfn[:], dfr[:], -1.0))
    v_op(lambda v: v.tensor_copy(lsif[:], lsii[:]))
    vdr()
    v_op(lambda v: v.tensor_scalar(smt[:, 0:4], lsif[:], bvt[0:1, 0:1], None, ALU.is_le))
    vdr()
    v_op(lambda v: v.reduce_sum(smt[:, 4:5], smt[:, 0:4], AX.X))
    vdr()
    v_op(lambda v: v.tensor_scalar(smt[:, 5:6], smt[:, 4:5], 1.0, None, ALU.subtract))
    vdr()
    v_op(lambda v: v.tensor_scalar(smt[:, 6:7], smt[:, 5:6], 0.0, None, ALU.is_gt))
    v_op(lambda v: v.tensor_scalar(smt[:, 7:8], smt[:, 5:6], 1.0, 0.0, ALU.subtract, ALU.max))
    vdr()
    v_op(lambda v: v.tensor_scalar(smt[:, 7:8], smt[:, 7:8], 2.0, None, ALU.min))
    vdr()
    for kk in range(3):
        v_op(lambda v, k_=kk: v.tensor_scalar(smt[:, k_:k_ + 1], smt[:, 7:8], float(k_),
                                              None, ALU.is_equal))
        vdr()
        v_op(lambda v, k_=kk: v.tensor_scalar(smt[:, k_:k_ + 1], smt[:, k_:k_ + 1],
                                              smt[:, 6:7], None, ALU.mult), inc=(kk == 2))
        vdr()
    p.wait("tensor", vec, vec.n)
    p.wait("tensor", rm, rm.n)
    mm(ps[0][0:16, 0:3], ones16[0:1, :], smt[0:1, 0:3], True, True, inc=True)
    p.wait("vector", pe, pe.n)
    v_op(lambda v: v.tensor_copy(s16[:], ps[0][0:16, 0:3]), inc=True)
    fence()

    def h_build(dst, x0buf, hbuf, with_stats, invs, bmnis, save, tiles, junk,
                tokn, pre_fills=()):
        hb_gate = [0, 0]
        for idx, tg in enumerate(tiles):
            slot = idx % 2
            if idx < len(pre_fills):
                fill = pre_fills[idx]
            else:
                fill = rx.load(slot, [(x0buf[:, slot * H:(slot + 1) * H],
                                       dx[tg * 128:(tg + 1) * 128, :])])
            xs = x0buf[:, slot * H:(slot + 1) * H]
            p.wait("vector", rx.sems[slot], fill)
            if with_stats:
                p.wait("scalar", rx.sems[slot], fill)
                ln_stats(xs, junk)
                if save:
                    v_op(lambda v, c=tg: v.tensor_copy(mean1[:, c:c + 1], stat[:, 2:3]))
                    v_op(lambda v, c=tg: v.tensor_copy(inv1[:, c:c + 1], stat[:, 5:6]))
                    v_op(lambda v, c=tg: v.tensor_copy(bmni1[:, c:c + 1], stat[:, 6:7]))
            hslot = idx % 2
            hb = hbuf[:, hslot * H:(hslot + 1) * H]
            if hb_gate[hslot] > 0:
                p.wait("vector", pe, hb_gate[hslot])
            if with_stats:
                v_op(lambda v, hh=hb, xx=xs: v.tensor_scalar(
                    hh, xx, stat[:, 2:3], stat[:, 5:6], ALU.add, ALU.mult), inc=True)
            else:
                v_op(lambda v, hh=hb, xx=xs, iv=invs[:, tg:tg + 1], bm=bmnis[:, tg:tg + 1]:
                     v.tensor_scalar(hh, xx, iv, bm, ALU.mult, ALU.add), inc=True)
            rx.release(slot, vec, vec.n)
            p.wait("tensor", vec, vec.n)
            for b in range(8):
                bank_wait("tensor", b)
                for j in range(4):
                    kk = b * 4 + j
                    def tr(t, o=ps[b][:, j * 128:(j + 1) * 128],
                           i_=hb[:, kk * 128:(kk + 1) * 128]):
                        return t.transpose(o, i_, ident[:])
                    if j < 3:
                        p.on("tensor", tr)
                    else:
                        pe.bump()
                        p.on("tensor", lambda t, f=tr, sh=pe.h: f(t).then_inc(sh, 1))
                p.wait("scalar", pe, pe.n)
                toff = idx * 128 if tokn == TOKH else tg * 128
                def ev(a, d_=dst, b_=b, to=toff, tn=tokn,
                       src=ps[b][:].rearrange("p (j t) -> p j t", j=4)):
                    dv = d_[:].rearrange("p (k t) -> p k t", t=tn)[:, b_ * 4:(b_ + 1) * 4,
                                                                  to:to + 128]
                    return a.activation(dv, src, AF.Copy)
                a_op(ev, inc=True)
                bank_ev[b] = (act, act.n)
            hb_gate[hslot] = pe.n

    def h_build_b0(tg, idx, x2buf):
        xs = x2buf[:, tg * H:(tg + 1) * H]
        ln_stats(xs, junkB, sums=x2sum[:, tg:tg + 1])
        if b0_gate[0] > 0:
            p.wait("vector", pe, b0_gate[0])
        v_op(lambda v, xx=xs: v.tensor_scalar(
            h2tok[:], xx, stat[:, 2:3], stat[:, 5:6], ALU.add, ALU.mult), inc=True)
        p.wait("tensor", vec, vec.n)
        for b in range(8):
            bank_wait("tensor", b)
            for j in range(4):
                kk = b * 4 + j
                def tr(t, o=ps[b][:, j * 128:(j + 1) * 128],
                       i_=h2tok[:, kk * 128:(kk + 1) * 128]):
                    return t.transpose(o, i_, ident[:])
                if j < 3:
                    p.on("tensor", tr)
                else:
                    pe.bump()
                    p.on("tensor", lambda t, f=tr, sh=pe.h: f(t).then_inc(sh, 1))
            p.wait("scalar", pe, pe.n)
            def ev(a, b_=b, to=idx * 128,
                   src=ps[b][:].rearrange("p (j t) -> p j t", j=4)):
                dv = h2T[:].rearrange("p (k t) -> p k t", t=TOKH)[:, b_ * 4:(b_ + 1) * 4,
                                                                 to:to + 128]
                return a.activation(dv, src, AF.Copy)
            a_op(ev, inc=True)
            bank_ev[b] = (act, act.n)
        b0_gate[0] = pe.n
        p.wait("tensor", rm, rm_b)
        p.wait("tensor", act, act.n)
        for a_i in range(3):
            bb = 4 + a_i
            bank_wait("tensor", bb)
            for k in range(KC):
                mm(ps[bb][0:16, 0:128], lAt[:, (a_i * KC + k) * 16:(a_i * KC + k + 1) * 16],
                   h2T[:, k * TOKH + idx * 128: k * TOKH + (idx + 1) * 128],
                   k == 0, k == KC - 1, inc=(k == KC - 1))
            p.wait("vector", pe, pe.n)
            v_op(lambda v, a_=a_i, b_=bb, i_=idx: v.tensor_scalar(
                t1s[:, a_ * 512 + i_ * 128: a_ * 512 + (i_ + 1) * 128],
                ps[b_][0:16, 0:128], s16[:, a_:a_ + 1], None, ALU.mult), inc=True)
            bank_ev[bb] = (vec, vec.n)
        t1_done[idx] = vec.n

    h_build(h_T, x0t, htok, True, None, None, True, list(range(TT)), junkA, R)
    rw.gate[0].append((vec, vec.n))
    a1_pref = rw.load(0, [(wupg[:, 0:H], dup[0]), (wupg[:, H:2 * H], dgt[0])])
    fence()

    dt_gate = [0, 0]
    for mS in range(SC):
        slot = mS % 2
        if mS == 0:
            wf = a1_pref
        else:
            wf = rw.load(slot, [
                (wupg[:, (2 * slot) * H:(2 * slot + 1) * H], dup[mS]),
                (wupg[:, (2 * slot + 1) * H:(2 * slot + 2) * H], dgt[mS]),
            ])
        par = (mS % 2) * 4
        bau = [par + 0, par + 1]
        bgt = [par + 2, par + 3]
        p.wait("tensor", rw.sems[slot], wf)
        for b in bau[:NHALF] + bgt[:NHALF]:
            bank_wait("tensor", b)
        wup_ap = wupg[:, (2 * slot) * H:(2 * slot + 1) * H]
        wgt_ap = wupg[:, (2 * slot + 1) * H:(2 * slot + 2) * H]
        for k in range(KC):
            for hi in range(NHALF):
                mm(ps[bau[hi]][:, 0:TL], wup_ap[:, k * 128:(k + 1) * 128],
                   h_T[:, k * R + hi * TL: k * R + (hi + 1) * TL],
                   k == 0, k == KC - 1, inc=(k == KC - 1))
            for hi in range(NHALF):
                mm(ps[bgt[hi]][:, 0:TL], wgt_ap[:, k * 128:(k + 1) * 128],
                   h_T[:, k * R + hi * TL: k * R + (hi + 1) * TL],
                   k == 0, k == KC - 1, inc=(k == KC - 1))
        rw.release(slot, pe, pe.n)
        p.wait("vector", pe, pe.n)
        for hi in range(NHALF):
            v_op(lambda v, b=bau[hi], o=au[:, mS * R + hi * TL: mS * R + (hi + 1) * TL]:
                 v.tensor_copy(o, ps[b][:, 0:TL]))
        dsl = mS % 2
        dta = dtile[:, dsl * R:(dsl + 1) * R]
        if dt_gate[dsl] > 0:
            p.wait("vector", act, dt_gate[dsl])
        for hi in range(NHALF):
            v_op(lambda v, b=bgt[hi], o=dta[:, hi * TL:(hi + 1) * TL], m=mS:
                 v.tensor_scalar(o, ps[b][:, 0:TL], lp[:, m:m + 1], None, ALU.subtract),
                 inc=(hi == NHALF - 1))
        for b in bau[:NHALF] + bgt[:NHALF]:
            bank_ev[b] = (vec, vec.n)
        p.wait("scalar", vec, vec.n)
        a_op(lambda a, d=dta: a.activation(dabs[:, 0:R], d, AF.Abs), inc=True)
        dt_gate[dsl] = act.n
        v_op(lambda v, d=dta: v.tensor_scalar(sgnt[:, 0:R], d, 0.0, 2.0, ALU.is_gt, ALU.mult))
        v_op(lambda v, m=mS: v.scalar_tensor_tensor(
            aus[:, m * R:(m + 1) * R], sgnt[:, 0:R], 1.0,
            au[:, m * R:(m + 1) * R], ALU.subtract, ALU.mult))
        p.wait("vector", act, act.n)
        v_op(lambda v: v.tensor_scalar(dabs[:, 0:R], dabs[:, 0:R], CLAMP, None, ALU.max),
             inc=True)
        p.wait("scalar", vec, vec.n)
        a_op(lambda a, m=mS: a.activation(logd[:, m * R:(m + 1) * R], dabs[:, 0:R], AF.Ln),
             inc=True)
    pe_a1_done = pe.n
    fuse_fill = {}
    for li in range(NFR - 1):
        rf.gate[li % NFR].append((pe, pe_a1_done))
        fuse_fill[li] = rf.load(li % NFR, [(frng[:, (li % NFR) * 3584:((li % NFR) + 1) * 3584],
                                            dfuse[li])])
    fence()

    bank_ev = [None] * 8
    bank_rd = [0] * 8
    tb_gate = [0, 0, 0]
    dt_gate2 = [0, 0, 0]
    kk_gate = [0]
    tcount = [0]
    for n in range(NCH):
        for k in range(SC):
            li = n * SC + k
            slot = li % NFR
            pre = li + NFR - 1
            if pre < NCH * SC and pre not in fuse_fill:
                fuse_fill[pre] = rf.load(pre % NFR,
                                         [(frng[:, (pre % NFR) * 3584:((pre % NFR) + 1) * 3584],
                                           dfuse[pre])])
            fill = fuse_fill[li]
            for g in range(7):
                if g == 0:
                    src = au[:, k * R:(k + 1) * R]
                    p.wait("tensor", rf.sems[slot], fill)
                else:
                    m = g
                    tsl = tcount[0] % 3
                    tcount[0] += 1
                    dts = dterm[:, tsl * R:(tsl + 1) * R]
                    tbs = tmpb[:, tsl * R:(tsl + 1) * R]
                    if dt_gate2[tsl] > 0:
                        p.wait("scalar", vec, dt_gate2[tsl])
                    if m == 5:
                        if kk_gate[0] > 0:
                            p.wait("scalar", vec, kk_gate[0])
                        a_op(lambda a, i_=logd[:, k * R:(k + 1) * R]:
                             a.activation(dabs2[:], i_, AF.Exp), inc=True)
                    a_op(lambda a, o=dts, i_=logd[:, k * R:(k + 1) * R], m_=m, g_=g:
                         a.activation(o, i_, AF.Exp, bias=dfn[:, g_ - 1:g_],
                                      scale=float(m_)), inc=True)
                    p.wait("vector", act, act.n)
                    if m == 5:
                        v_op(lambda v: v.tensor_scalar(keepk[:], dabs2[:], THD, None,
                                                       ALU.is_le))
                    if tb_gate[tsl] > 0:
                        p.wait("vector", pe, tb_gate[tsl])
                    bv = aus if (m % 2 == 1) else au
                    v_op(lambda v, o=tbs, d=dts, b_=bv[:, k * R:(k + 1) * R]:
                         v.tensor_tensor(o, d, b_, ALU.mult), inc=(m <= 4))
                    if m > 4:
                        v_op(lambda v, o=tbs:
                             v.tensor_tensor(o, o, keepk[:], ALU.mult), inc=True)
                        if m == 6:
                            kk_gate[0] = vec.n
                    dt_gate2[tsl] = vec.n
                    src = tbs
                    p.wait("tensor", vec, vec.n)
                last_gk = (k == SC - 1 and g == 6)
                for m_t in range(TT):
                    if k == 0 and g == 0:
                        bank_wait("tensor", m_t)
                    mm(ps[m_t][:], src[:, m_t * 128:(m_t + 1) * 128],
                       frng[:, slot * 3584 + g * 512: slot * 3584 + (g + 1) * 512],
                       (k == 0 and g == 0), last_gk,
                       inc=(last_gk or m_t == TT - 1))
                    if last_gk:
                        bank_rd[m_t] = pe.n
                if g > 0:
                    tb_gate[tsl] = pe.n
            rf.release(slot, pe, pe.n)
        for m_t in range(TT):
            p.wait("vector", pe, bank_rd[m_t])
            v_op(lambda v, b=m_t, n_=n:
                 v.tensor_copy(ya[:, b * H + n_ * 512:(b * H + (n_ + 1) * 512)], ps[b][:]),
                 inc=True)
            bank_ev[m_t] = (vec, vec.n)
    pf = []
    for i2 in range(min(2, TT)):
        rx.gate[i2 % 2].append((pe, pe.n))
        pf.append(rx.load(i2 % 2, [(x0t2[:, (i2 % 2) * H:((i2 % 2) + 1) * H],
                                    dx[i2 * 128:(i2 + 1) * 128, :])]))
    fence()

    bank_ev = [None] * 8
    h_build(h_T2, x0t2, htok2, False, inv1, bmni1, False, list(range(TT)), junkA, R,
            pre_fills=pf)
    pe_hb = pe.n
    rw.gate[0].append((pe, pe_hb))
    wgu_pref = rw.load(0, [
        (wgu[:, 0:H], dgf[0]),
        (wgu[:, H:2 * H], duf[0]),
    ])
    fence()

    fc0 = 0
    rel0_pe = [0]
    rd_rel0 = [0]
    rd_pref = [None]
    for ig, gsz in enumerate(FGROUPS):
        gs_gate = [0, 0]
        for mf in range(gsz):
            slot = mf % 2
            if mf == 0 and wgu_pref is not None:
                wf = wgu_pref
                wgu_pref = None
            else:
                wf = rw.load(slot, [
                    (wgu[:, (2 * slot) * H:(2 * slot + 1) * H], dgf[fc0 + mf]),
                    (wgu[:, (2 * slot + 1) * H:(2 * slot + 2) * H], duf[fc0 + mf]),
                ])
            par = (mf % 2) * 2
            bg = [par + 0, par + 1]
            bu = [par + 4, par + 5]
            p.wait("tensor", rw.sems[slot], wf)
            for b in bg[:NHALF] + bu[:NHALF]:
                bank_wait("tensor", b)
            wg_ap = wgu[:, (2 * slot) * H:(2 * slot + 1) * H]
            wu_ap = wgu[:, (2 * slot + 1) * H:(2 * slot + 2) * H]
            for k in range(KC):
                for hi in range(NHALF):
                    mm(ps[bg[hi]][:, 0:TL], wg_ap[:, k * 128:(k + 1) * 128],
                       h_T2[:, k * R + hi * TL: k * R + (hi + 1) * TL],
                       k == 0, k == KC - 1, inc=(k == KC - 1))
            peg_g = pe.n
            for k in range(KC):
                for hi in range(NHALF):
                    mm(ps[bu[hi]][:, 0:TL], wu_ap[:, k * 128:(k + 1) * 128],
                       h_T2[:, k * R + hi * TL: k * R + (hi + 1) * TL],
                       k == 0, k == KC - 1, inc=(k == KC - 1))
            rw.release(slot, pe, pe.n)
            if slot == 0:
                rel0_pe[0] = pe.n
            peg = pe.n
            gsl = mf % 2
            if gs_gate[gsl] > 0:
                p.wait("scalar", vec, gs_gate[gsl])
            p.wait("scalar", pe, peg_g)
            for hi in range(NHALF):
                a_op(lambda a, b=bg[hi], o=gst[:, gsl * R + hi * TL: gsl * R + (hi + 1) * TL]:
                     a.activation(o, ps[b][:, 0:TL], AF.Silu), inc=(hi == NHALF - 1))
            p.wait("vector", act, act.n)
            p.wait("vector", pe, peg)
            for hi in range(NHALF):
                v_op(lambda v, b=bu[hi], m=mf, h_=hi, g_=gsl:
                     v.scalar_tensor_tensor(
                         vg[:, m * R + h_ * TL: m * R + (h_ + 1) * TL],
                         ps[b][:, 0:TL], 1.0,
                         gst[:, g_ * R + h_ * TL: g_ * R + (h_ + 1) * TL],
                         ALU.mult, ALU.mult),
                     inc=(hi == NHALF - 1))
            gs_gate[gsl] = vec.n
            for b in bg[:NHALF] + bu[:NHALF]:
                bank_ev[b] = (vec, vec.n)
        rd.gate[0].append((pe, rel0_pe[0]))
        rd_pref[0] = rd.load(0, [(drng[:, 0: gsz * 512],
                                  ddn[0][:, fc0 * 512:(fc0 + gsz) * 512])])
        fence()
        for n in range(NCH):
            slot = n % 2
            if n == 0:
                fill = rd_pref[0]
            else:
                fill = rd.load(slot, [(drng[:, slot * 8192: slot * 8192 + gsz * 512],
                                       ddn[n][:, fc0 * 512:(fc0 + gsz) * 512])])
            p.wait("tensor", rd.sems[slot], fill)
            for kf in range(gsz):
                for m_t in range(TT):
                    if kf == 0:
                        bank_wait("tensor", m_t)
                    mm(ps[m_t][:], vg[:, kf * R + m_t * 128: kf * R + (m_t + 1) * 128],
                       drng[:, slot * 8192 + kf * 512: slot * 8192 + (kf + 1) * 512],
                       kf == 0, kf == gsz - 1,
                       inc=(kf == gsz - 1 or m_t == TT - 1))
                    if kf == gsz - 1:
                        bank_rd[m_t] = pe.n
            rd.release(slot, pe, pe.n)
            last_grp = (ig == len(FGROUPS) - 1)
            for m_t in range(TT):
                p.wait("vector", pe, bank_rd[m_t])
                v_op(lambda v, b=m_t, n_=n, lg=last_grp:
                     v.scalar_tensor_tensor(
                         ya[:, b * H + n_ * 512: b * H + (n_ + 1) * 512],
                         ps[b][:], 1.0,
                         ya[:, b * H + n_ * 512: b * H + (n_ + 1) * 512],
                         ALU.mult, ALU.add,
                         accum_out=(yasum[:, b * NCH + n_: b * NCH + n_ + 1]
                                    if lg else None)),
                     inc=True)
                bank_ev[m_t] = (vec, vec.n)
            if n == NCH - 2:
                rd_rel0[0] = pe.n
        fc0 += gsz
        if ig < len(FGROUPS) - 1:
            rw.gate[0].append((pe, rd_rel0[0]))
            wgu_pref = rw.load(0, [
                (wgu[:, 0:H], dgf[fc0]),
                (wgu[:, H:2 * H], duf[fc0]),
            ])
        fence()
    pe_p2 = pe.n

    for dst, src in ((lAt[:], dlA), (biasb[:], dbias)):
        def fn(g, d=dst, s=src, sh=rm.h):
            g.dma_start(out=d, in_=s).then_inc(sh, 16)
        p.on("gpsimd", fn)
        rm.bump(16)
    rm_b = rm.n
    t1_done = [0] * THALF

    x2 = ya
    bank_ev = [None] * 8
    b0_gate = [0]
    x3_gate = [0]

    def p3_tile(t):
        fill = rx.load(0, [(x0t3[:], dx[t * 128:(t + 1) * 128, :])])
        yat = ya[:, t * H:(t + 1) * H]
        ln_stats(yat, junkB, sums=yasum[:, t * NCH:(t + 1) * NCH])
        p.wait("vector", rx.sems[0], fill)
        v_op(lambda v: v.tensor_scalar(x0t3[:], x0t3[:], stat[:, 6:7], None, ALU.add))
        vdr()
        v_op(lambda v, y=yat, t_=t: v.scalar_tensor_tensor(
            y, y, stat[:, 5:6], x0t3[:], ALU.mult, ALU.add,
            accum_out=x2sum[:, t_:t_ + 1]), inc=True)
        x3_gate[0] = vec.n
        rx.release(0, vec, vec.n)

    for t in range(THALF):
        p3_tile(t)
        h_build_b0(t, t, x2)

    b2_last = [0]
    for hf in range(NHALF):
        tls = list(range(hf * THALF, (hf + 1) * THALF))
        rb.gate[1].append((pe, pe_p2) if hf == 0 else (vec, b2_last[0]))
        rb.gate[0].append((pe, b0_gate[0]))
        bank_ev_b = [bank_ev[b] for b in range(8)]
        first_evac = True
        p3_next = [NHALF == 2 and hf == 0, THALF]
        for n in range(NCH):
            kfill = rk2.load(n % 2, [(bkr[:, (n % 2) * 1536:(n % 2) * 1536 + 1536],
                                      dlB.rearrange("p (a h) -> p a h", a=3)[:, :, n * 512:(n + 1) * 512])])
            for khalf in range(2):
                slot = khalf ^ 1
                fill = rb.load(slot, [(wslB[:, slot * 8192:(slot + 1) * 8192],
                                       dW[n][:, khalf * 8192:(khalf + 1) * 8192])])
                p.wait("tensor", rb.sems[slot], fill)
                for m_t in range(THALF):
                    b = m_t + (n % 2) * 4
                    if n == 0 and khalf == 0:
                        p.wait("tensor", vec, t1_done[m_t])
                    if khalf == 0 and bank_ev_b[b] is not None:
                        p.waitt("tensor", bank_ev_b[b])
                    for k in range(16):
                        kg = khalf * 16 + k
                        mm(ps[b][:], h2T[:, kg * TOKH + m_t * 128: kg * TOKH + (m_t + 1) * 128],
                           wslB[:, slot * 8192 + k * 512: slot * 8192 + (k + 1) * 512],
                           kg == 0, False,
                           inc=(m_t == THALF - 1 and k == 15))
                rb.release(slot, pe, pe.n)
            p.wait("tensor", rk2.sems[n % 2], kfill)
            for m_t in range(THALF):
                b = m_t + (n % 2) * 4
                for a_i in range(3):
                    mm(ps[b][:], t1s[0:16, a_i * 512 + m_t * 128: a_i * 512 + (m_t + 1) * 128],
                       bkr[0:16, (n % 2) * 1536 + a_i * 512:(n % 2) * 1536 + (a_i + 1) * 512],
                       False, a_i == 2,
                       inc=(a_i == 2))
                bank_rd[b] = pe.n
            rk2.release(n % 2, pe, pe.n)
            if first_evac:
                p.wait("vector", rm, rm_b)
                first_evac = False
            for m_t in range(THALF):
                b = m_t + (n % 2) * 4
                p.wait("vector", pe, bank_rd[b])
                v_op(lambda v, b_=b, m=m_t, n_=n:
                     v.scalar_tensor_tensor(
                         y2h[:, m * H + n_ * 512: m * H + (n_ + 1) * 512],
                         ps[b_][:], 1.0, biasb[:, n_ * 512:(n_ + 1) * 512],
                         ALU.mult, ALU.add,
                         accum_out=y2sum[:, m * NCH + n_: m * NCH + n_ + 1]),
                     inc=True)
                bank_ev_b[b] = (vec, vec.n)
            if p3_next[0] and n % 2 == 0 and p3_next[1] < TT:
                p3_tile(p3_next[1])
                p3_next[1] += 1
        y2_done = vec.n
        bank_ev = [None] * 8
        b0_gate = [0]
        wsl_done = pe.n
        for idx, tg in enumerate(tls):
            y2t = y2h[:, idx * H:(idx + 1) * H]
            ln_stats(y2t, junkB, sums=y2sum[:, idx * NCH:(idx + 1) * NCH])
            v_op(lambda v, y=y2t, xs=x2[:, tg * H:(tg + 1) * H]:
                 v.tensor_scalar(tt4[:], xs, stat[:, 6:7], None, ALU.add))
            vdr()
            if ro.sems[0].n > 0:
                p.wait("vector", ro.sems[0], ro.sems[0].n)
            v_op(lambda v, y=y2t:
                 v.scalar_tensor_tensor(outt[:], y, stat[:, 5:6], tt4[:],
                                        ALU.mult, ALU.add), inc=True)
            b2_last[0] = vec.n
            p.wait("sync", vec, vec.n)
            sem = ro.sems[0]

            def st(e, o=dout[tg * 128:(tg + 1) * 128, :], i_=outt[:], sh=sem.h):
                e.dma_start(out=o, in_=i_).then_inc(sh, 16)
            p.on("sync", st)
            sem.bump(16)
            if hf == 0 and NHALF == 2:
                if idx == 0:
                    p.wait("vector", pe, wsl_done)
                h_build_b0(THALF + idx, idx, x2)

    p.wait("sync", ro.sems[0], ro.sems[0].n)

    p.emit(nc)
    return nc



def _prep_shared(i):
    up_w = np.asarray(i["L0_attn_up_w"], np.float32)
    gate_w = np.asarray(i["L0_attn_gate_w"], np.float32)
    lao = np.asarray(i["L0_attn_local_approx_output"], np.float32)
    fuse = np.asarray(i["L0_attn_fuse_weight"], np.float32)
    gfw = np.asarray(i["L0_attn_ffn_gate_w"], np.float32)
    ufw = np.asarray(i["L0_attn_ffn_up_w"], np.float32)
    dfw = np.asarray(i["L0_attn_ffn_down_w"], np.float32)
    W = np.asarray(i["L0_ffn_weight"], np.float32)
    bias = np.asarray(i["L0_ffn_bias"], np.float32)
    lA = np.asarray(i["L0_ffn_lora_A"], np.float32)
    lB = np.asarray(i["L0_ffn_lora_B"], np.float32)
    lsi = np.asarray(i["L0_ffn_lora_start_indices"]).astype(np.int32).reshape(1, 4)
    lp = np.asarray(i["L0_attn_local_point"], np.float32)
    df = np.asarray(i["L0_attn_discount_factor"], np.float32)

    d = {}
    d["up_h"] = np.ascontiguousarray(
        up_w.reshape(SC, 128, KC, 128).transpose(0, 3, 2, 1).reshape(SC, 128, KC * 128))
    d["gate_h"] = np.ascontiguousarray(
        gate_w.reshape(SC, 128, KC, 128).transpose(0, 3, 2, 1).reshape(SC, 128, KC * 128))
    F_all = np.stack([lao.T] + [fuse[:, :, g].T for g in range(6)])
    d["fuse_h"] = np.ascontiguousarray(
        F_all.reshape(7, SC, 128, NCH, 512).transpose(3, 1, 2, 0, 4)
        .reshape(NCH * SC, 128, 7 * 512))
    d["gatef_h"] = np.ascontiguousarray(
        gfw.reshape(FC, 128, KC, 128).transpose(0, 3, 2, 1).reshape(FC, 128, KC * 128))
    d["upf_h"] = np.ascontiguousarray(
        ufw.reshape(FC, 128, KC, 128).transpose(0, 3, 2, 1).reshape(FC, 128, KC * 128))
    d["down_h"] = np.ascontiguousarray(
        dfw.T.reshape(FC, 128, NCH, 512).transpose(2, 1, 0, 3).reshape(NCH, 128, FC * 512))
    d["W_h"] = np.ascontiguousarray(
        W.T.reshape(KC, 128, NCH, 512).transpose(2, 1, 0, 3).reshape(NCH, 128, KC * 512))
    d["bias_bc"] = np.ascontiguousarray(np.broadcast_to(bias, (128, H)))
    d["lA_h"] = np.ascontiguousarray(
        lA.reshape(3, 16, KC, 128).transpose(3, 0, 2, 1).reshape(128, 3 * KC * 16))
    d["lB_h"] = np.ascontiguousarray(lB.transpose(2, 0, 1).reshape(16, 3 * H))
    d["lsi"] = lsi
    d["ident"] = np.eye(128, dtype=np.float32)
    d["lp_h"] = np.ascontiguousarray(lp.reshape(SC, 128).T)
    d["df_h"] = np.ascontiguousarray(np.broadcast_to(df, (128, 6)))
    return d


def make_in_maps(inputs, n_cores, rows_per_core):
    x = np.asarray(inputs["L0_attn_hidden_states"], np.float32)
    B, T, _ = x.shape
    xf = np.ascontiguousarray(x.reshape(B * T, H))
    shared = _prep_shared(inputs)
    in_maps = []
    for c in range(n_cores):
        m = dict(shared)
        r0 = c * rows_per_core
        m["x"] = np.ascontiguousarray(xf[r0:r0 + rows_per_core])
        m["bvec"] = np.full((1, 1), float(r0 // T), np.float32)
        in_maps.append(m)
    return in_maps


def kernel(**inputs):
    n_cores = 8
    R = 1024
    in_maps = make_in_maps(inputs, n_cores, R)
    nc = build_program(R)
    res = run_bass_kernel_spmd(nc, in_maps, core_ids=list(range(n_cores)))
    out = np.concatenate([np.asarray(res.results[c]["out"]) for c in range(n_cores)],
                         axis=0)
    return out.reshape(2, 4096, 4096).astype(np.float32, copy=False)
